# version 01
# baseline (speedup 1.0000x reference)
import contextlib
import numpy as np
import ml_dtypes
import concourse.bass as bass
import concourse.mybir as mybir
from concourse.bass_utils import run_bass_kernel_spmd

F32 = mybir.dt.float32
BF16 = mybir.dt.bfloat16
I32 = mybir.dt.int32
AX = mybir.AxisListType.X
OP = mybir.AluOpType
AF = mybir.ActivationFunctionType
BF = ml_dtypes.bfloat16

B, S, D = 2, 4096, 4096
T = B * S
NT = T // 128
NK = D // 128
NCORE = 8
SCALE0 = 1.0 / np.sqrt(128.0)
SCALE1 = 1.0 / np.sqrt(192.0)
NEG = -1e30


class Ser:

    def __init__(self):
        self.ops = []

    def E(self, engine, method, *args, inc=1, **kwargs):
        self.ops.append(
            (engine,
             lambda eng, m=method, a=args, k=kwargs: getattr(eng, m)(*a, **k),
             inc))

    def dma(self, out, in_, engine="sync"):
        self.E(engine, "dma_start", inc=16, out=out, in_=in_)

    def build(self, nc, sem):
        used = []
        for e, _, _ in self.ops:
            if e not in used:
                used.append(e)
        total = sum(i for _, _, i in self.ops)
        with nc.Block() as block:
            for ename in used:
                dec = getattr(block, ename)

                def section(eng, ename=ename):
                    cum = 0
                    for e, fn, inc in self.ops:
                        if e == ename:
                            eng.wait_ge(sem, cum)
                            fn(eng).then_inc(sem, inc)
                        cum += inc
                    eng.wait_ge(sem, total)

                dec(section)


def _bassnc():
    return bass.Bass("TRN2", target_bir_lowering=False)


def emit_layernorm(s, xt, xc, tmp, r0, r1, xnb):
    s.E("vector", "reduce_sum", r0[:], xt[:], axis=AX)
    s.E("vector", "tensor_scalar_mul", r1[:], r0[:], 1.0 / D)
    s.E("vector", "tensor_scalar", out=xc[:], in0=xt[:], scalar1=r1[:, 0:1],
        scalar2=None, op0=OP.subtract)
    s.E("vector", "tensor_mul", tmp[:], xc[:], xc[:])
    s.E("vector", "reduce_sum", r0[:], tmp[:], axis=AX)
    s.E("vector", "tensor_scalar", out=r0[:], in0=r0[:], scalar1=1.0 / D,
        scalar2=1e-5, op0=OP.mult, op1=OP.add)
    s.E("scalar", "activation", out=r1[:], in_=r0[:], func=AF.Sqrt)
    s.E("vector", "reciprocal", r0[:], r1[:])
    s.E("vector", "tensor_scalar", out=xnb[:], in0=xc[:], scalar1=r0[:, 0:1],
        scalar2=None, op0=OP.mult)


def emit_dequant_fp4(s, nc_codes_ap, sc_ap, k, code_i, code_f, srow, scb, w_dst,
                     ones_f, ncols):
    s.dma(code_i[:, 0:ncols], nc_codes_ap[k * 128:(k + 1) * 128, :])
    s.dma(srow[0:1, 0:ncols], sc_ap[k:k + 1, :])
    s.E("vector", "tensor_copy", out=code_f[:, 0:ncols], in_=code_i[:, 0:ncols])
    s.E("vector", "tensor_scalar", out=code_f[:, 0:ncols],
        in0=code_f[:, 0:ncols], scalar1=-8.0, scalar2=0.5,
        op0=OP.add, op1=OP.mult)
    for g0 in range(0, ncols, 512):
        w = min(512, ncols - g0)
        s.E("tensor", "matmul", scb[:, g0:g0 + w], ones_f[0:1, 0:128],
            srow[0:1, g0:g0 + w], start=True, stop=True)
    s.E("vector", "tensor_tensor", out=w_dst, in0=scb[:, 0:ncols],
        in1=code_f[:, 0:ncols], op=OP.mult)


def build_ln_ffn0():
    nc = _bassnc()
    xs = nc.dram_tensor("xs", [1024, D], F32, kind="ExternalInput").ap()
    rwT = nc.dram_tensor("rwT", [D, 4], F32, kind="ExternalInput").ap()
    aT = nc.dram_tensor("aT", [D, 64], F32, kind="ExternalInput").ap()
    bT = nc.dram_tensor("bT", [64, D], F32, kind="ExternalInput").ap()
    ident = nc.dram_tensor("ident", [128, 128], BF16, kind="ExternalInput").ap()
    xnT_o = nc.dram_tensor("xnT_o", [D, 1024], BF16, kind="ExternalOutput").ap()
    part_o = nc.dram_tensor("part_o", [1024, D], F32, kind="ExternalOutput").ap()

    s = Ser()
    with contextlib.ExitStack() as ctx:
        sb = lambda n, sh, dt: ctx.enter_context(nc.sbuf_tensor(n, sh, dt))
        ps = lambda n, sh, dt: ctx.enter_context(nc.psum_tensor(n, sh, dt))
        xt = sb("xt", [128, D], F32)
        xc = sb("xc", [128, D], F32)
        tmp = sb("tmp", [128, D], F32)
        xnb = sb("xnb", [128, D], BF16)
        xnT = sb("xnT", [128, D], BF16)
        rw_f = sb("rw_f", [128, 128], F32)
        rw_b = sb("rw_b", [128, 128], BF16)
        a_f = sb("a_f", [128, 2048], F32)
        a_b = sb("a_b", [128, 2048], BF16)
        b_f = sb("b_f", [64, D], F32)
        b_b = sb("b_b", [64, D], BF16)
        idb = sb("idb", [128, 128], BF16)
        r0 = sb("r0", [128, 1], F32)
        r1 = sb("r1", [128, 1], F32)
        l_sb = sb("l_sb", [128, 4], F32)
        eqm = sb("eqm", [128, 4], F32)
        msk = sb("msk", [128, 4], F32)
        keep = sb("keep", [128, 4], F32)
        ex = sb("ex", [128, 4], F32)
        eg = sb("eg", [128, 4], F32)
        gts = sb("gts", [128, 4], F32)
        m1 = sb("m1", [128, 1], F32)
        m2 = sb("m2", [128, 1], F32)
        nm1 = sb("nm1", [128, 1], F32)
        den = sb("den", [128, 1], F32)
        rden = sb("rden", [128, 1], F32)
        mg = sb("mg", [128, 64], F32)
        mgb = sb("mgb", [128, 64], BF16)
        mgT = sb("mgT", [64, 128], BF16)
        part = sb("part", [128, D], F32)
        tp = ps("tp", [128, 128], BF16)
        lg = ps("lg", [128, 4], F32)
        mid = ps("mid", [128, 64], F32)
        mgTp = ps("mgTp", [128, 128], BF16)
        dp = ps("dp", [128, 512], F32)
        sem = ctx.enter_context(nc.semaphore("S"))

        s.dma(idb[:], ident[:])
        for k in range(NK):
            s.dma(rw_f[:, 4 * k:4 * k + 4], rwT[k * 128:(k + 1) * 128, :])
            s.dma(a_f[:, 64 * k:64 * k + 64], aT[k * 128:(k + 1) * 128, :])
        s.E("vector", "tensor_copy", out=rw_b[:], in_=rw_f[:])
        s.E("vector", "tensor_copy", out=a_b[:], in_=a_f[:])
        s.dma(b_f[:], bT[:])
        s.E("vector", "tensor_copy", out=b_b[:], in_=b_f[:])

        for t in range(8):
            s.dma(xt[:], xs[t * 128:(t + 1) * 128, :])
            emit_layernorm(s, xt, xc, tmp, r0, r1, xnb)
            for k in range(NK):
                s.E("tensor", "transpose", tp[:], xnb[:, k * 128:(k + 1) * 128],
                    idb[:])
                s.E("scalar", "copy", out=xnT[:, k * 128:(k + 1) * 128],
                    in_=tp[:])
                s.dma(xnT_o[k * 128:(k + 1) * 128, t * 128:(t + 1) * 128],
                      xnT[:, k * 128:(k + 1) * 128])
            for k in range(NK):
                s.E("tensor", "matmul", lg[:], xnT[:, k * 128:(k + 1) * 128],
                    rw_b[:, 4 * k:4 * k + 4], start=(k == 0), stop=(k == NK - 1))
            for k in range(NK):
                s.E("tensor", "matmul", mid[:], xnT[:, k * 128:(k + 1) * 128],
                    a_b[:, 64 * k:64 * k + 64], start=(k == 0), stop=(k == NK - 1))
            s.E("scalar", "copy", out=l_sb[:], in_=lg[:])
            s.E("vector", "reduce_max", m1[:], l_sb[:], axis=AX)
            s.E("vector", "tensor_scalar", out=eqm[:], in0=l_sb[:],
                scalar1=m1[:, 0:1], scalar2=None, op0=OP.is_equal)
            s.E("vector", "scalar_tensor_tensor", out=msk[:], in0=eqm[:],
                scalar=NEG, in1=l_sb[:], op0=OP.mult, op1=OP.add)
            s.E("vector", "reduce_max", m2[:], msk[:], axis=AX)
            s.E("vector", "tensor_scalar", out=keep[:], in0=l_sb[:],
                scalar1=m2[:, 0:1], scalar2=None, op0=OP.is_ge)
            s.E("vector", "tensor_scalar_mul", nm1[:], m1[:], -1.0)
            s.E("scalar", "activation", out=ex[:], in_=l_sb[:], func=AF.Exp,
                bias=nm1[:, 0:1], scale=1.0)
            s.E("vector", "tensor_mul", eg[:], ex[:], keep[:])
            s.E("vector", "reduce_sum", den[:], eg[:], axis=AX)
            s.E("vector", "reciprocal", rden[:], den[:])
            s.E("vector", "tensor_scalar", out=gts[:], in0=eg[:],
                scalar1=rden[:, 0:1], scalar2=None, op0=OP.mult)
            for e in range(4):
                s.E("vector", "tensor_scalar", out=mg[:, 16 * e:16 * e + 16],
                    in0=mid[:, 16 * e:16 * e + 16], scalar1=gts[:, e:e + 1],
                    scalar2=4.0, op0=OP.mult, op1=OP.mult)
            s.E("vector", "tensor_copy", out=mgb[:], in_=mg[:])
            s.E("tensor", "transpose", mgTp[0:64, :], mgb[:], idb[:])
            s.E("scalar", "copy", out=mgT[:], in_=mgTp[0:64, :])
            for cc in range(8):
                s.E("tensor", "matmul", dp[:], mgT[:],
                    b_b[:, 512 * cc:512 * cc + 512], start=True, stop=True)
                s.E("vector", "tensor_tensor",
                    out=part[:, 512 * cc:512 * cc + 512], in0=dp[:],
                    in1=xt[:, 512 * cc:512 * cc + 512], op=OP.add)
            s.dma(part_o[t * 128:(t + 1) * 128, :], part[:])
        s.build(nc, sem)
    return nc


def build_qkv0():
    nc = _bassnc()
    qc = nc.dram_tensor("qc", [D, 512], I32, kind="ExternalInput").ap()
    kc = nc.dram_tensor("kc", [D, 128], I32, kind="ExternalInput").ap()
    vc = nc.dram_tensor("vc", [D, 128], I32, kind="ExternalInput").ap()
    qsc = nc.dram_tensor("qsc", [NK, 512], F32, kind="ExternalInput").ap()
    ksc = nc.dram_tensor("ksc", [NK, 128], F32, kind="ExternalInput").ap()
    vsc = nc.dram_tensor("vsc", [NK, 128], F32, kind="ExternalInput").ap()
    xnT = nc.dram_tensor("xnT", [D, T], BF16, kind="ExternalInput").ap()
    cosT = nc.dram_tensor("cosT", [T, 64], F32, kind="ExternalInput").ap()
    sinT = nc.dram_tensor("sinT", [T, 64], F32, kind="ExternalInput").ap()
    ident = nc.dram_tensor("ident", [128, 128], BF16, kind="ExternalInput").ap()
    onesf = nc.dram_tensor("onesf", [1, 128], F32, kind="ExternalInput").ap()
    qT_o = nc.dram_tensor("qT_o", [512, T], BF16, kind="ExternalOutput").ap()
    kT_o = nc.dram_tensor("kT_o", [128, T], BF16, kind="ExternalOutput").ap()
    v_o = nc.dram_tensor("v_o", [T, 128], BF16, kind="ExternalOutput").ap()

    s = Ser()
    with contextlib.ExitStack() as ctx:
        sb = lambda n, sh, dt: ctx.enter_context(nc.sbuf_tensor(n, sh, dt))
        ps = lambda n, sh, dt: ctx.enter_context(nc.psum_tensor(n, sh, dt))
        w_sb = sb("w_sb", [128, NK, 768], BF16)
        code_i = sb("code_i", [128, 768], I32)
        code_f = sb("code_f", [128, 768], F32)
        srow = sb("srow", [1, 768], F32)
        ones = sb("ones", [1, 128], F32)
        idb = sb("idb", [128, 128], BF16)
        xe = sb("xe", [128, NK, 128], BF16)
        ct = sb("ct", [128, 64], F32)
        st = sb("st", [128, 64], F32)
        t1 = sb("t1", [128, 64], F32)
        t2 = sb("t2", [128, 64], F32)
        qro = sb("qro", [128, 512], BF16)
        kro = sb("kro", [128, 128], BF16)
        vro = sb("vro", [128, 128], BF16)
        trs = sb("trs", [128, 128], BF16)
        scb = ps("scb", [128, 768], F32)
        qp = ps("qp", [128, 512], F32)
        kp = ps("kp", [128, 128], F32)
        vp = ps("vp", [128, 128], F32)
        tpp = ps("tpp", [128, 128], BF16)
        sem = ctx.enter_context(nc.semaphore("S"))

        s.dma(idb[:], ident[:])
        s.dma(ones[:], onesf[:])
        for k in range(NK):
            s.dma(code_i[:, 0:512], qc[k * 128:(k + 1) * 128, :])
            s.dma(code_i[:, 512:640], kc[k * 128:(k + 1) * 128, :])
            s.dma(code_i[:, 640:768], vc[k * 128:(k + 1) * 128, :])
            s.dma(srow[0:1, 0:512], qsc[k:k + 1, :])
            s.dma(srow[0:1, 512:640], ksc[k:k + 1, :])
            s.dma(srow[0:1, 640:768], vsc[k:k + 1, :])
            s.E("vector", "tensor_copy", out=code_f[:], in_=code_i[:])
            s.E("vector", "tensor_scalar", out=code_f[:], in0=code_f[:],
                scalar1=-8.0, scalar2=0.5, op0=OP.add, op1=OP.mult)
            s.E("tensor", "matmul", scb[:, 0:512], ones[0:1, :],
                srow[0:1, 0:512], start=True, stop=True)
            s.E("tensor", "matmul", scb[:, 512:768], ones[0:1, :],
                srow[0:1, 512:768], start=True, stop=True)
            s.E("vector", "tensor_tensor", out=w_sb[:, k, :], in0=scb[:],
                in1=code_f[:], op=OP.mult)

        for t in range(NT):
            for k in range(NK):
                s.dma(xe[:, k, :], xnT[k * 128:(k + 1) * 128,
                                       t * 128:(t + 1) * 128])
            s.dma(ct[:], cosT[t * 128:(t + 1) * 128, :])
            s.dma(st[:], sinT[t * 128:(t + 1) * 128, :])
            for k in range(NK):
                s.E("tensor", "matmul", qp[:], xe[:, k, :], w_sb[:, k, 0:512],
                    start=(k == 0), stop=(k == NK - 1))
            for k in range(NK):
                s.E("tensor", "matmul", kp[:], xe[:, k, :], w_sb[:, k, 512:640],
                    start=(k == 0), stop=(k == NK - 1))
            for k in range(NK):
                s.E("tensor", "matmul", vp[:], xe[:, k, :], w_sb[:, k, 640:768],
                    start=(k == 0), stop=(k == NK - 1))
            for h in range(4):
                c0 = h * 128
                s.E("vector", "tensor_mul", t1[:], qp[:, c0:c0 + 64], ct[:])
                s.E("vector", "tensor_mul", t2[:], qp[:, c0 + 64:c0 + 128], st[:])
                s.E("vector", "tensor_sub", qro[:, c0:c0 + 64], t1[:], t2[:])
                s.E("vector", "tensor_mul", t1[:], qp[:, c0 + 64:c0 + 128], ct[:])
                s.E("vector", "tensor_mul", t2[:], qp[:, c0:c0 + 64], st[:])
                s.E("vector", "tensor_add", qro[:, c0 + 64:c0 + 128], t1[:], t2[:])
            s.E("vector", "tensor_mul", t1[:], kp[:, 0:64], ct[:])
            s.E("vector", "tensor_mul", t2[:], kp[:, 64:128], st[:])
            s.E("vector", "tensor_sub", kro[:, 0:64], t1[:], t2[:])
            s.E("vector", "tensor_mul", t1[:], kp[:, 64:128], ct[:])
            s.E("vector", "tensor_mul", t2[:], kp[:, 0:64], st[:])
            s.E("vector", "tensor_add", kro[:, 64:128], t1[:], t2[:])
            s.E("vector", "tensor_copy", out=vro[:], in_=vp[:])
            for h in range(4):
                s.E("tensor", "transpose", tpp[:], qro[:, h * 128:(h + 1) * 128],
                    idb[:])
                s.E("scalar", "copy", out=trs[:], in_=tpp[:])
                s.dma(qT_o[h * 128:(h + 1) * 128, t * 128:(t + 1) * 128], trs[:])
            s.E("tensor", "transpose", tpp[:], kro[:], idb[:])
            s.E("scalar", "copy", out=trs[:], in_=tpp[:])
            s.dma(kT_o[0:128, t * 128:(t + 1) * 128], trs[:])
            s.dma(v_o[t * 128:(t + 1) * 128, :], vro[:])
        s.build(nc, sem)
    return nc


def build_attn(kv_group, scale):
    nkv = 8 // kv_group
    nc = _bassnc()
    qT = nc.dram_tensor("qT", [1024, S], BF16, kind="ExternalInput").ap()
    kT = nc.dram_tensor("kT", [nkv * 128, S], BF16, kind="ExternalInput").ap()
    vv = nc.dram_tensor("vv", [S, nkv * 128], BF16, kind="ExternalInput").ap()
    dmask = nc.dram_tensor("dmask", [128, 128], F32, kind="ExternalInput").ap()
    ident = nc.dram_tensor("ident", [128, 128], BF16, kind="ExternalInput").ap()
    aT_o = nc.dram_tensor("aT_o", [1024, S], BF16, kind="ExternalOutput").ap()

    NQ = S // 128
    s = Ser()
    with contextlib.ExitStack() as ctx:
        sb = lambda n, sh, dt: ctx.enter_context(nc.sbuf_tensor(n, sh, dt))
        ps = lambda n, sh, dt: ctx.enter_context(nc.psum_tensor(n, sh, dt))
        qh = sb("qh", [128, S], BF16)
        kh = sb("kh", [128, S], BF16)
        vh = sb("vh", [128, NQ, 128], BF16)
        ss = sb("ss", [128, S], F32)
        pb = sb("pb", [128, S], BF16)
        pT = sb("pT", [128, S], BF16)
        msk = sb("msk", [128, 128], F32)
        idb = sb("idb", [128, 128], BF16)
        m = sb("m", [128, 1], F32)
        nm = sb("nm", [128, 1], F32)
        sume = sb("sume", [128, 1], F32)
        rsum = sb("rsum", [128, 1], F32)
        ob = sb("ob", [128, 128], BF16)
        obT = sb("obT", [128, 128], BF16)
        sc = ps("sc", [128, 512], F32)
        ptp = ps("ptp", [128, 128], BF16)
        av = ps("av", [128, 128], F32)
        otp = ps("otp", [128, 128], BF16)
        sem = ctx.enter_context(nc.semaphore("S"))

        s.dma(msk[:], dmask[:])
        s.dma(idb[:], ident[:])
        for h in range(8):
            s.dma(qh[:], qT[h * 128:(h + 1) * 128, :])
            if h % kv_group == 0:
                kv = h // kv_group
                s.dma(kh[:], kT[kv * 128:(kv + 1) * 128, :])
                for i in range(NQ):
                    s.dma(vh[:, i, :],
                          vv[i * 128:(i + 1) * 128, kv * 128:(kv + 1) * 128])
            for qt in range(NQ):
                nch = qt + 1
                L = 128 * nch
                for g0 in range(0, L, 512):
                    w = min(512, L - g0)
                    s.E("tensor", "matmul", sc[:, 0:w],
                        qh[:, qt * 128:(qt + 1) * 128], kh[:, g0:g0 + w],
                        start=True, stop=True)
                    hasdiag = (g0 + w == L)
                    pre = w - 128 if hasdiag else w
                    if pre > 0:
                        s.E("scalar", "mul", out=ss[:, g0:g0 + pre],
                            in_=sc[:, 0:pre], mul=scale)
                    if hasdiag:
                        s.E("vector", "scalar_tensor_tensor",
                            out=ss[:, L - 128:L], in0=sc[:, w - 128:w],
                            scalar=scale, in1=msk[:], op0=OP.mult, op1=OP.add)
                s.E("vector", "reduce_max", m[:], ss[:, 0:L], axis=AX)
                s.E("vector", "tensor_scalar_mul", nm[:], m[:], -1.0)
                s.E("scalar", "activation", out=pb[:, 0:L], in_=ss[:, 0:L],
                    func=AF.Exp, bias=nm[:, 0:1], scale=1.0)
                s.E("vector", "reduce_sum", sume[:], pb[:, 0:L], axis=AX)
                s.E("vector", "reciprocal", rsum[:], sume[:])
                for i in range(nch):
                    s.E("tensor", "transpose", ptp[:],
                        pb[:, i * 128:(i + 1) * 128], idb[:])
                    s.E("scalar", "copy", out=pT[:, i * 128:(i + 1) * 128],
                        in_=ptp[:])
                for i in range(nch):
                    s.E("tensor", "matmul", av[:], pT[:, i * 128:(i + 1) * 128],
                        vh[:, i, :], start=(i == 0), stop=(i == nch - 1))
                s.E("vector", "tensor_scalar", out=ob[:], in0=av[:],
                    scalar1=rsum[:, 0:1], scalar2=None, op0=OP.mult)
                s.E("tensor", "transpose", otp[:], ob[:], idb[:])
                s.E("scalar", "copy", out=obT[:], in_=otp[:])
                s.dma(aT_o[h * 128:(h + 1) * 128, qt * 128:(qt + 1) * 128],
                      obT[:])
        s.build(nc, sem)
    return nc


def build_oproj0():
    nc = _bassnc()
    oc = nc.dram_tensor("oc", [D, 512], I32, kind="ExternalInput").ap()
    osc = nc.dram_tensor("osc", [NK, 512], F32, kind="ExternalInput").ap()
    aT = nc.dram_tensor("aT", [D, T], BF16, kind="ExternalInput").ap()
    prt = nc.dram_tensor("prt", [T, 512], F32, kind="ExternalInput").ap()
    onesf = nc.dram_tensor("onesf", [1, 128], F32, kind="ExternalInput").ap()
    x1_o = nc.dram_tensor("x1_o", [T, 512], F32, kind="ExternalOutput").ap()

    s = Ser()
    with contextlib.ExitStack() as ctx:
        sb = lambda n, sh, dt: ctx.enter_context(nc.sbuf_tensor(n, sh, dt))
        ps = lambda n, sh, dt: ctx.enter_context(nc.psum_tensor(n, sh, dt))
        w_sb = sb("w_sb", [128, NK, 512], BF16)
        code_i = sb("code_i", [128, 512], I32)
        code_f = sb("code_f", [128, 512], F32)
        srow = sb("srow", [1, 512], F32)
        ones = sb("ones", [1, 128], F32)
        xe = sb("xe", [128, NK, 128], BF16)
        pt = sb("pt", [128, 512], F32)
        ot = sb("ot", [128, 512], F32)
        scb = ps("scb", [128, 512], F32)
        mp = ps("mp", [128, 512], F32)
        sem = ctx.enter_context(nc.semaphore("S"))

        s.dma(ones[:], onesf[:])
        for k in range(NK):
            emit_dequant_fp4(s, oc, osc, k, code_i, code_f, srow, scb,
                             w_sb[:, k, :], ones, 512)
        for t in range(NT):
            for k in range(NK):
                s.dma(xe[:, k, :], aT[k * 128:(k + 1) * 128,
                                      t * 128:(t + 1) * 128])
            s.dma(pt[:], prt[t * 128:(t + 1) * 128, :])
            for k in range(NK):
                s.E("tensor", "matmul", mp[:], xe[:, k, :], w_sb[:, k, :],
                    start=(k == 0), stop=(k == NK - 1))
            s.E("vector", "tensor_tensor", out=ot[:], in0=mp[:], in1=pt[:],
                op=OP.add)
            s.dma(x1_o[t * 128:(t + 1) * 128, :], ot[:])
        s.build(nc, sem)
    return nc


def build_ln1():
    nc = _bassnc()
    xs = nc.dram_tensor("xs", [1024, D], F32, kind="ExternalInput").ap()
    ident = nc.dram_tensor("ident", [128, 128], BF16, kind="ExternalInput").ap()
    xnT_o = nc.dram_tensor("xnT_o", [D, 1024], BF16, kind="ExternalOutput").ap()
    rs_o = nc.dram_tensor("rs_o", [1024, 1], F32, kind="ExternalOutput").ap()

    s = Ser()
    with contextlib.ExitStack() as ctx:
        sb = lambda n, sh, dt: ctx.enter_context(nc.sbuf_tensor(n, sh, dt))
        ps = lambda n, sh, dt: ctx.enter_context(nc.psum_tensor(n, sh, dt))
        xt = sb("xt", [128, D], F32)
        xc = sb("xc", [128, D], F32)
        tmp = sb("tmp", [128, D], F32)
        xnb = sb("xnb", [128, D], BF16)
        xnT = sb("xnT", [128, D], BF16)
        idb = sb("idb", [128, 128], BF16)
        r0 = sb("r0", [128, 1], F32)
        r1 = sb("r1", [128, 1], F32)
        rs = sb("rs", [128, 1], F32)
        tp = ps("tp", [128, 128], BF16)
        sem = ctx.enter_context(nc.semaphore("S"))

        s.dma(idb[:], ident[:])
        for t in range(8):
            s.dma(xt[:], xs[t * 128:(t + 1) * 128, :])
            emit_layernorm(s, xt, xc, tmp, r0, r1, xnb)
            s.E("vector", "reduce_sum", rs[:], xnb[:], axis=AX)
            s.dma(rs_o[t * 128:(t + 1) * 128, :], rs[:])
            for k in range(NK):
                s.E("tensor", "transpose", tp[:], xnb[:, k * 128:(k + 1) * 128],
                    idb[:])
                s.E("scalar", "copy", out=xnT[:, k * 128:(k + 1) * 128],
                    in_=tp[:])
                s.dma(xnT_o[k * 128:(k + 1) * 128, t * 128:(t + 1) * 128],
                      xnT[:, k * 128:(k + 1) * 128])
        s.build(nc, sem)
    return nc


def build_qkv1():
    nc = _bassnc()
    wqs = nc.dram_tensor("wqs", [D, 512], F32, kind="ExternalInput").ap()
    wks = nc.dram_tensor("wks", [D, 512], F32, kind="ExternalInput").ap()
    wvs = nc.dram_tensor("wvs", [D, 512], F32, kind="ExternalInput").ap()
    xnT = nc.dram_tensor("xnT", [D, T], BF16, kind="ExternalInput").ap()
    ident = nc.dram_tensor("ident", [128, 128], BF16, kind="ExternalInput").ap()
    qT_o = nc.dram_tensor("qT_o", [512, T], BF16, kind="ExternalOutput").ap()
    kT_o = nc.dram_tensor("kT_o", [512, T], BF16, kind="ExternalOutput").ap()
    v_o = nc.dram_tensor("v_o", [T, 512], BF16, kind="ExternalOutput").ap()

    s = Ser()
    with contextlib.ExitStack() as ctx:
        sb = lambda n, sh, dt: ctx.enter_context(nc.sbuf_tensor(n, sh, dt))
        ps = lambda n, sh, dt: ctx.enter_context(nc.psum_tensor(n, sh, dt))
        w_sb = sb("w_sb", [128, NK, 1536], BF16)
        wst = sb("wst", [128, 1536], F32)
        idb = sb("idb", [128, 128], BF16)
        xe = sb("xe", [128, NK, 128], BF16)
        qb = sb("qb", [128, 512], BF16)
        kb = sb("kb", [128, 512], BF16)
        vb = sb("vb", [128, 512], BF16)
        trs = sb("trs", [128, 128], BF16)
        qp = ps("qp", [128, 512], F32)
        kp = ps("kp", [128, 512], F32)
        vp = ps("vp", [128, 512], F32)
        tpp = ps("tpp", [128, 128], BF16)
        sem = ctx.enter_context(nc.semaphore("S"))

        s.dma(idb[:], ident[:])
        for k in range(NK):
            s.dma(wst[:, 0:512], wqs[k * 128:(k + 1) * 128, :])
            s.dma(wst[:, 512:1024], wks[k * 128:(k + 1) * 128, :])
            s.dma(wst[:, 1024:1536], wvs[k * 128:(k + 1) * 128, :])
            s.E("vector", "tensor_copy", out=w_sb[:, k, :], in_=wst[:])
        for t in range(NT):
            for k in range(NK):
                s.dma(xe[:, k, :], xnT[k * 128:(k + 1) * 128,
                                       t * 128:(t + 1) * 128])
            for k in range(NK):
                s.E("tensor", "matmul", qp[:], xe[:, k, :], w_sb[:, k, 0:512],
                    start=(k == 0), stop=(k == NK - 1))
            for k in range(NK):
                s.E("tensor", "matmul", kp[:], xe[:, k, :], w_sb[:, k, 512:1024],
                    start=(k == 0), stop=(k == NK - 1))
            for k in range(NK):
                s.E("tensor", "matmul", vp[:], xe[:, k, :], w_sb[:, k, 1024:1536],
                    start=(k == 0), stop=(k == NK - 1))
            s.E("vector", "tensor_copy", out=qb[:], in_=qp[:])
            s.E("vector", "tensor_copy", out=kb[:], in_=kp[:])
            s.E("vector", "tensor_copy", out=vb[:], in_=vp[:])
            for h in range(4):
                s.E("tensor", "transpose", tpp[:], qb[:, h * 128:(h + 1) * 128],
                    idb[:])
                s.E("scalar", "copy", out=trs[:], in_=tpp[:])
                s.dma(qT_o[h * 128:(h + 1) * 128, t * 128:(t + 1) * 128], trs[:])
            for h in range(4):
                s.E("tensor", "transpose", tpp[:], kb[:, h * 128:(h + 1) * 128],
                    idb[:])
                s.E("scalar", "copy", out=trs[:], in_=tpp[:])
                s.dma(kT_o[h * 128:(h + 1) * 128, t * 128:(t + 1) * 128], trs[:])
            s.dma(v_o[t * 128:(t + 1) * 128, :], vb[:])
        s.build(nc, sem)
    return nc


def build_final():
    nc = _bassnc()
    wos = nc.dram_tensor("wos", [D, 512], F32, kind="ExternalInput").ap()
    qws = nc.dram_tensor("qws", [512, 512], I32, kind="ExternalInput").ap()
    scs = nc.dram_tensor("scs", [1, 512], F32, kind="ExternalInput").ap()
    zrs = nc.dram_tensor("zrs", [1, 512], F32, kind="ExternalInput").ap()
    bis = nc.dram_tensor("bis", [1, 512], F32, kind="ExternalInput").ap()
    rsr = nc.dram_tensor("rsr", [1, T], F32, kind="ExternalInput").ap()
    aT = nc.dram_tensor("aT", [D, T], BF16, kind="ExternalInput").ap()
    xnT = nc.dram_tensor("xnT", [D, T], BF16, kind="ExternalInput").ap()
    x1s = nc.dram_tensor("x1s", [T, 512], F32, kind="ExternalInput").ap()
    shf = nc.dram_tensor("shf", [128, 1], I32, kind="ExternalInput").ap()
    onesf = nc.dram_tensor("onesf", [1, 128], F32, kind="ExternalInput").ap()
    out_o = nc.dram_tensor("out_o", [T, 512], F32, kind="ExternalOutput").ap()

    s = Ser()
    with contextlib.ExitStack() as ctx:
        sb = lambda n, sh, dt: ctx.enter_context(nc.sbuf_tensor(n, sh, dt))
        ps = lambda n, sh, dt: ctx.enter_context(nc.psum_tensor(n, sh, dt))
        wo_sb = sb("wo_sb", [128, NK, 512], BF16)
        q_sb = sb("q_sb", [128, NK, 512], BF16)
        wst = sb("wst", [128, 512], F32)
        qi = sb("qi", [128, 512], I32)
        sh_sb = sb("sh_sb", [128, 1], I32)
        s_sb = sb("s_sb", [1, 512], F32)
        z_sb = sb("z_sb", [1, 512], F32)
        b_sb = sb("b_sb", [1, 512], F32)
        nz_f = sb("nz_f", [1, 512], F32)
        nz_b = sb("nz_b", [1, 512], BF16)
        rs_f = sb("rs_f", [1, T], F32)
        rs_b = sb("rs_b", [1, T], BF16)
        ones = sb("ones", [1, 128], F32)
        sct = sb("sct", [128, 512], F32)
        bbt = sb("bbt", [128, 512], F32)
        xa = sb("xa", [128, NK, 128], BF16)
        xn = sb("xn", [128, NK, 128], BF16)
        x1t = sb("x1t", [128, 512], F32)
        ot = sb("ot", [128, 512], F32)
        ot2 = sb("ot2", [128, 512], F32)
        Ap = ps("Ap", [128, 512], F32)
        Bp = ps("Bp", [128, 512], F32)
        bc = ps("bc", [128, 512], F32)
        sem = ctx.enter_context(nc.semaphore("S"))

        s.dma(ones[:], onesf[:])
        s.dma(sh_sb[:], shf[:])
        s.dma(s_sb[:], scs[:])
        s.dma(z_sb[:], zrs[:])
        s.dma(b_sb[:], bis[:])
        s.dma(rs_f[:], rsr[:])
        s.E("vector", "tensor_copy", out=rs_b[:], in_=rs_f[:])
        s.E("scalar", "mul", out=nz_f[:], in_=z_sb[:], mul=-1.0)
        s.E("vector", "tensor_copy", out=nz_b[:], in_=nz_f[:])
        s.E("tensor", "matmul", bc[:], ones[0:1, :], s_sb[0:1, :],
            start=True, stop=True)
        s.E("scalar", "copy", out=sct[:], in_=bc[:])
        s.E("tensor", "matmul", bc[:], ones[0:1, :], b_sb[0:1, :],
            start=True, stop=True)
        s.E("scalar", "copy", out=bbt[:], in_=bc[:])
        for k in range(NK):
            s.dma(wst[:], wos[k * 128:(k + 1) * 128, :])
            s.E("vector", "tensor_copy", out=wo_sb[:, k, :], in_=wst[:])
            for n in range(8):
                s.dma(qi[n:128:8, :], qws[k * 16:(k + 1) * 16, :])
            s.E("vector", "tensor_scalar", out=qi[:], in0=qi[:],
                scalar1=sh_sb[:, 0:1], scalar2=None, op0=OP.logical_shift_right)
            s.E("vector", "tensor_scalar", out=qi[:], in0=qi[:], scalar1=15,
                scalar2=None, op0=OP.bitwise_and)
            s.E("vector", "tensor_copy", out=q_sb[:, k, :], in_=qi[:])
        for t in range(NT):
            for k in range(NK):
                s.dma(xa[:, k, :], aT[k * 128:(k + 1) * 128,
                                      t * 128:(t + 1) * 128])
                s.dma(xn[:, k, :], xnT[k * 128:(k + 1) * 128,
                                       t * 128:(t + 1) * 128])
            s.dma(x1t[:], x1s[t * 128:(t + 1) * 128, :])
            for k in range(NK):
                s.E("tensor", "matmul", Ap[:], xn[:, k, :], q_sb[:, k, :],
                    start=(k == 0), stop=(k == NK - 1))
            for k in range(NK):
                s.E("tensor", "matmul", Bp[:], xa[:, k, :], wo_sb[:, k, :],
                    start=(k == 0), stop=False)
            s.E("tensor", "matmul", Bp[:], rs_b[0:1, t * 128:(t + 1) * 128],
                nz_b[0:1, :], start=False, stop=True)
            s.E("vector", "tensor_tensor", out=ot[:], in0=Ap[:], in1=sct[:],
                op=OP.mult)
            s.E("vector", "tensor_tensor", out=ot2[:], in0=Bp[:], in1=ot[:],
                op=OP.add)
            s.E("vector", "tensor_add", ot2[:], ot2[:], x1t[:])
            s.E("vector", "tensor_add", ot2[:], ot2[:], bbt[:])
            s.dma(out_o[t * 128:(t + 1) * 128, :], ot2[:])
        s.build(nc, sem)
    return nc


def _run(nc, in_maps):
    res = run_bass_kernel_spmd(nc, in_maps, core_ids=list(range(NCORE)))
    return res.results


def _cc(a):
    return np.ascontiguousarray(a)


def kernel(L0_attn_x, L0_attn_q_codes, L0_attn_k_codes, L0_attn_v_codes,
           L0_attn_o_codes, L0_attn_q_scales, L0_attn_k_scales,
           L0_attn_v_scales, L0_attn_o_scales, L0_ffn_router_w, L0_ffn_A,
           L0_ffn_B, L1_attn_wq, L1_attn_wkv, L1_attn_wo, L1_ffn_qweight,
           L1_ffn_scales, L1_ffn_zeros, L1_ffn_bias):
    X = np.ascontiguousarray(L0_attn_x.reshape(T, D).astype(np.float32))
    ident = np.eye(128).astype(BF)
    onesf = np.ones((1, 128), dtype=np.float32)
    pos = (np.arange(T) % S).astype(np.float64)
    inv = 1.0 / (10000.0 ** (np.arange(64, dtype=np.float64) / 64.0))
    ang = pos[:, None] * inv[None, :]
    cosT = np.cos(ang).astype(np.float32)
    sinT = np.sin(ang).astype(np.float32)
    dmask = np.where(np.arange(128)[None, :] <= np.arange(128)[:, None],
                     0.0, NEG).astype(np.float32)
    shf = ((np.arange(128) % 8) * 4).astype(np.int32).reshape(128, 1)

    rwT = _cc(L0_ffn_router_w.T.astype(np.float32))
    aT = _cc(L0_ffn_A.reshape(64, D).T.astype(np.float32))
    bT = _cc(L0_ffn_B.transpose(0, 2, 1).reshape(64, D).astype(np.float32))
    nc = build_ln_ffn0()
    maps = [{"xs": _cc(X[c * 1024:(c + 1) * 1024]), "rwT": rwT, "aT": aT,
             "bT": bT, "ident": ident} for c in range(NCORE)]
    res = _run(nc, maps)
    xnT = np.concatenate([r["xnT_o"] for r in res], axis=1)
    part = np.concatenate([r["part_o"] for r in res], axis=0)

    nc = build_qkv0()
    maps = []
    for c in range(NCORE):
        maps.append({
            "qc": _cc(L0_attn_q_codes[:, 512 * c:512 * (c + 1)]),
            "kc": _cc(L0_attn_k_codes[:, 128 * c:128 * (c + 1)]),
            "vc": _cc(L0_attn_v_codes[:, 128 * c:128 * (c + 1)]),
            "qsc": _cc(L0_attn_q_scales[:, 512 * c:512 * (c + 1)]),
            "ksc": _cc(L0_attn_k_scales[:, 128 * c:128 * (c + 1)]),
            "vsc": _cc(L0_attn_v_scales[:, 128 * c:128 * (c + 1)]),
            "xnT": xnT, "cosT": cosT, "sinT": sinT, "ident": ident,
            "onesf": onesf})
    res = _run(nc, maps)
    qT0 = np.concatenate([r["qT_o"] for r in res], axis=0)
    kT0 = np.concatenate([r["kT_o"] for r in res], axis=0)
    v0 = np.concatenate([r["v_o"] for r in res], axis=1)

    nc = build_attn(4, SCALE0)
    maps = []
    for c in range(NCORE):
        b, h0 = c // 4, 8 * (c % 4)
        kv0 = h0 // 4
        maps.append({
            "qT": _cc(qT0[h0 * 128:(h0 + 8) * 128, b * S:(b + 1) * S]),
            "kT": _cc(kT0[kv0 * 128:(kv0 + 2) * 128, b * S:(b + 1) * S]),
            "vv": _cc(v0[b * S:(b + 1) * S, kv0 * 128:(kv0 + 2) * 128]),
            "dmask": dmask, "ident": ident})
    res = _run(nc, maps)
    a0T = np.empty((D, T), dtype=BF)
    for c in range(NCORE):
        b, h0 = c // 4, 8 * (c % 4)
        a0T[h0 * 128:(h0 + 8) * 128, b * S:(b + 1) * S] = res[c]["aT_o"]

    nc = build_oproj0()
    maps = []
    for c in range(NCORE):
        maps.append({
            "oc": _cc(L0_attn_o_codes[:, 512 * c:512 * (c + 1)]),
            "osc": _cc(L0_attn_o_scales[:, 512 * c:512 * (c + 1)]),
            "aT": a0T, "prt": _cc(part[:, 512 * c:512 * (c + 1)]),
            "onesf": onesf})
    res = _run(nc, maps)
    x1 = np.concatenate([r["x1_o"] for r in res], axis=1)

    nc = build_ln1()
    maps = [{"xs": _cc(x1[c * 1024:(c + 1) * 1024]), "ident": ident}
            for c in range(NCORE)]
    res = _run(nc, maps)
    xn1T = np.concatenate([r["xnT_o"] for r in res], axis=1)
    rs_row = _cc(np.concatenate([r["rs_o"] for r in res], axis=0).reshape(1, T))

    wq_eff = L1_attn_wq.reshape(D, 32, 192)[:, :, :128].reshape(D, D)
    nc = build_qkv1()
    maps = []
    for c in range(NCORE):
        maps.append({
            "wqs": _cc(wq_eff[:, 512 * c:512 * (c + 1)].astype(np.float32)),
            "wks": _cc(L1_attn_wkv[:, 512 * c:512 * (c + 1)].astype(np.float32)),
            "wvs": _cc(L1_attn_wkv[:, D + 512 * c:D + 512 * (c + 1)].astype(np.float32)),
            "xnT": xn1T, "ident": ident})
    res = _run(nc, maps)
    qT1 = np.concatenate([r["qT_o"] for r in res], axis=0)
    kT1 = np.concatenate([r["kT_o"] for r in res], axis=0)
    v1 = np.concatenate([r["v_o"] for r in res], axis=1)

    nc = build_attn(1, SCALE1)
    maps = []
    for c in range(NCORE):
        b, h0 = c // 4, 8 * (c % 4)
        maps.append({
            "qT": _cc(qT1[h0 * 128:(h0 + 8) * 128, b * S:(b + 1) * S]),
            "kT": _cc(kT1[h0 * 128:(h0 + 8) * 128, b * S:(b + 1) * S]),
            "vv": _cc(v1[b * S:(b + 1) * S, h0 * 128:(h0 + 8) * 128]),
            "dmask": dmask, "ident": ident})
    res = _run(nc, maps)
    a1T = np.empty((D, T), dtype=BF)
    for c in range(NCORE):
        b, h0 = c // 4, 8 * (c % 4)
        a1T[h0 * 128:(h0 + 8) * 128, b * S:(b + 1) * S] = res[c]["aT_o"]

    nc = build_final()
    maps = []
    for c in range(NCORE):
        cl = slice(512 * c, 512 * (c + 1))
        maps.append({
            "wos": _cc(L1_attn_wo[:, cl].astype(np.float32)),
            "qws": _cc(L1_ffn_qweight[:, cl]),
            "scs": _cc(L1_ffn_scales[cl, 0].reshape(1, 512).astype(np.float32)),
            "zrs": _cc(L1_ffn_zeros[cl, 0].reshape(1, 512).astype(np.float32)),
            "bis": _cc(L1_ffn_bias[cl].reshape(1, 512).astype(np.float32)),
            "rsr": rs_row, "aT": a1T, "xnT": xn1T,
            "x1s": _cc(x1[:, cl]), "shf": shf, "onesf": onesf})
    res = _run(nc, maps)
    out = np.concatenate([r["out_o"] for r in res], axis=1)
    return out.reshape(B, S, D).astype(np.float32)
